# version 18
# baseline (speedup 1.0000x reference)
import numpy as np
import ml_dtypes
import concourse.bass as bass
import concourse.mybir as mybir
from concourse.bass_utils import run_bass_kernel_spmd
from contextlib import ExitStack

F32 = mybir.dt.float32
BF16 = mybir.dt.bfloat16
AF = mybir.ActivationFunctionType
ALU = mybir.AluOpType
AX = mybir.AxisListType

NCORES = 8
T = 2048
C = 1024
NTT = 16
NCT = 8
E = 8
EPS = 1e-5
SCALE = 0.125
BIGC = 1.0e9
PHASES = 5
NBLK_RUN = 16

XN_F = 0
AUN4 = [0, 1024, 2048, 3072]
AMSK = 4096
ATB2 = [5120, 6144, 27648]
PRC = [28672, 30720]
LXB4 = [163840, 167936, 147456, 151552]
ATTN = 7168
OBUF = [11264, 15360]
XB = [19456, 23552]
G_SB = 32768
WEOFF = 65536
LXB = [163840, 167936, 172032]
XSL = [163840, 167936, 172032, 176128, 180224, 184320]
LSQ2 = 192512
XN32B = [172032, 176128, 180224]
XNF32B = [184320, 188416, 192512]
GV = 163840
M1A = GV; M2A = GV + 64; I1A = GV + 128; I2A = GV + 192; DA = GV + 256
W2A = GV + 320; W1A = GV + 384
LTA = GV + 448; PENA = GV + 960; EQ1A = GV + 1472; EQ2A = GV + 1984
Z2A = GV + 2496; WTA = GV + 3008
IOTR = GV + 3520; WBFA = GV + 4032
LSQ = 176128
XN32 = [180224, 184320]
XNF32 = [188416, 192512, 176128]
QOFF = 65536
KOFF = 98304
VOFF = 131072
WQT = [163840, 172032]
WPT = 180224
WBC = 196608
XNE = [198656, 200704]
IDENTF = 198656
WG32 = 199168
ZALL = 199424
R1A = 199936; R2A = 200000; INVA = 200064; NBA = 200128; LNT = 200192
SM0 = 202752
IDENT = SM0
B01 = SM0 + 256
SEL8 = SM0 + 512
IOTA = SM0 + 2560
WGOFF = SM0 + 2600
BQPP = SM0 + 2728
BE = SM0 + 2824
WT = SM0 + 4872
VROW = WT
PROW = WT + 2048
WALL = SM0 + 8968
WBF = SM0 + 9480
ZB = SM0 + 9496
GS = SM0 + 9528
M1 = GS; I1 = GS + 4; M2 = GS + 8; I2 = GS + 12; DD = GS + 16; W2V = GS + 20; W1V = GS + 24
LT = GS + 32; PEN = GS + 64; EQ1 = GS + 96; EQ2 = GS + 128; Z2 = GS + 160; WTMP = GS + 192
BQ8 = GS
MNEG2 = [GS + 32, GS + 36]
R8N = GS + 40
RHON = GS + 72
MST = GS + 224
R1S = MST; R2S = MST + 4; MEAN = MST + 8; VAR0 = MST + 12; MSQ = MST + 16
VAR = MST + 20; LNV = MST + 24; INV = MST + 28; NB = MST + 32
AST = MST + 36
MNEG = AST; BIASC = AST + 4; R8 = AST + 8; RHO = AST + 40
ARENA_END = AST + 72
ARENA_BYTES = 212736
assert ARENA_END <= ARENA_BYTES


def build_nc():
    nc = bass.Bass("TRN2", target_bir_lowering=False)
    di = {}

    def dram_in(name, shape, dt=F32):
        di[name] = nc.dram_tensor(name, shape, dt, kind="ExternalInput").ap()
        return di[name]

    x_d = dram_in("x", [T, C])
    wqkvT_d = dram_in("wqkvT", [C, 3 * C])
    bqpp_d = dram_in("bqpp", [128, 24])
    vrow_d = dram_in("vrow", [1, C])
    wprojT_d = dram_in("wprojT", [C, C])
    prow_d = dram_in("prow", [1, C])
    wg_d = dram_in("wg", [128, 64])
    we_d = dram_in("we", [E, 128, 8192])
    be_d = dram_in("be", [E, C])
    ident_d = dram_in("ident", [128, 128], BF16)
    identf_d = dram_in("identf", [128, 128])
    b01_d = dram_in("b01", [128, 128], BF16)
    sel8_d = dram_in("sel8", [8, 1024], BF16)
    iota_d = dram_in("iota", [128, 9])
    out_d = nc.dram_tensor("out", [T, C], F32, kind="ExternalOutput").ap()

    x_tiles = x_d.rearrange("(n p) c -> n p c", p=128)
    out_tiles = out_d.rearrange("(n p) c -> n p c", p=128)
    wq_r = wqkvT_d.rearrange("(ct p) o -> p ct o", p=128)
    wp_r = wprojT_d.rearrange("(ct p) o -> p ct o", p=128)

    es = ExitStack()
    ar_h = es.enter_context(nc.sbuf_tensor("arena", [128, ARENA_BYTES], mybir.dt.uint8))
    ar = ar_h[:]

    def vf(off, n, p0=0, p1=128):
        return ar[p0:p1, off:off + 4 * n].bitcast(F32)

    def vb(off, n, p0=0, p1=128):
        return ar[p0:p1, off:off + 2 * n].bitcast(BF16)

    xnf3 = vb(XN_F, 16384).rearrange("p (ct t) -> p ct t", t=T)
    gsb = vb(G_SB, 16384)
    ident = vb(IDENT, 128)
    identf = vf(IDENTF, 128)
    wg32 = vf(WG32, 64)
    zall = vf(ZALL, 128)
    b01 = vb(B01, 128)
    sel8 = vb(SEL8, 1024, 0, 8)
    ones_row = sel8[0:1, 0:128]
    iota9 = vf(IOTA, 9)
    iota8 = iota9[:, 0:8]
    epsc = iota9[:, 8:9]
    bqpp = vf(BQPP, 24)
    be_sb = vb(BE, 1024, 0, 8)
    wt_sb = vb(WT, 2048, 0, 8)
    vrow = vb(VROW, 1024, 0, 1)
    prow = vb(PROW, 1024, 0, 1)
    wall = vf(WALL, 128)
    wbf = vb(WBF, 8)
    zb = vf(ZB, 8)

    P = [es.enter_context(nc.psum_tensor(f"P{i}", [128, 1024], F32)) for i in range(4)]

    sems = {}
    for name in (["pe", "dve", "act", "sm", "x0", "x1", "x2", "x3", "x4", "x5", "xa", "xb",
                  "wg", "bes", "vr", "wq0", "wq1", "wp", "so0", "so1"]
                 + [f"we{e}" for e in range(8)]):
        sems[name] = es.enter_context(nc.semaphore(name))
    ctr = {k: 0 for k in sems}
    punit = {u: {} for u in range(8)}

    def bump(name, amt=1):
        ctr[name] += amt
        return ctr[name]

    def W(eng, name, val):
        if val and val > 0:
            eng.wait_ge(sems[name], val)

    def wait_units(eng, units):
        need = {}
        for u in units:
            for nm, v in punit[u].items():
                need[nm] = max(need.get(nm, 0), v)
        for nm, v in need.items():
            W(eng, nm, v)

    def touch(units, eng, val, reset=False):
        for u in units:
            if reset:
                punit[u] = {}
            punit[u][eng] = max(punit[u].get(eng, 0), val)

    blk = es.enter_context(nc.Block())

    SDEF = []

    def s_flush():
        if SDEF:
            @blk.scalar
            def _(s):
                for fn in SDEF:
                    fn(s)
            SDEF.clear()

    def finish_trunc():
        s_flush()

        @blk.sync
        def _(sy):
            for nm in sems:
                W(sy, nm, ctr[nm])
        es.close()
        return nc

    ev_wq = [16 * (g // 2 + 1) for g in range(6)]

    @blk.sync
    def _(sy):
        sy.dma_start(out=ident, in_=ident_d[:, :]).then_inc(sems["sm"], 16)
        sy.dma_start(out=b01, in_=b01_d[:, :]).then_inc(sems["sm"], 16)
        sy.dma_start(out=sel8, in_=sel8_d[:, :]).then_inc(sems["sm"], 16)
        sy.dma_start(out=iota9, in_=iota_d[:, :]).then_inc(sems["sm"], 16)
        sy.dma_start(out=bqpp, in_=bqpp_d[:, :]).then_inc(sems["sm"], 16)
        sy.dma_start(out=identf, in_=identf_d[:, :]).then_inc(sems["sm"], 16)
        sy.dma_start(out=wg32, in_=wg_d[:, :]).then_inc(sems["sm"], 16)
    ctr["sm"] = 112

    ev_apply = [NTT + 3] + [NTT + 2 + 2 * tt for tt in range(1, NTT)]
    ev_evac32 = [NTT + 5 + 2 * tt for tt in range(NTT - 1)] + [NTT + 2 + 2 * NTT]

    @blk.gpsimd
    def _(g):
        g.dma_start(out=be_sb, in_=be_d[:, :]).then_inc(sems["bes"], 16)
        W(g, "act", NTT)
        for e in range(5):
            g.dma_start(out=vb(WEOFF + e * 16384, 8192), in_=we_d[e]).then_inc(sems[f"we{e}"], 16)
    ctr["bes"] = 16
    for e in range(5):
        ctr[f"we{e}"] = 16

    def step_v(v, fn):
        fn().then_inc(sems["dve"], 1)
        val = bump("dve")
        W(v, "dve", val)
        return val

    e_xnf = [0] * NTT
    e_trl = [0] * NTT
    e_z0l = [0] * NTT
    e_zev = [0] * NTT
    xcnt = {k: 0 for k in range(6)}

    x_pairs = x_d.rearrange("(a p) c -> p a c", p=128)

    @blk.sync
    def _(sy):
        for pr in range(NTT // 2):
            tt = 2 * pr
            if tt >= 6:
                W(sy, "act", tt - 6 + 2)
                W(sy, "dve", tt - 6 + 2)
            sy.dma_start(out=vf(XSL[tt % 6], 2048), in_=x_pairs[:, tt:tt + 2, :]).then_inc(sems[f"x{tt % 6}"], 16)
    for pr in range(NTT // 2):
        xcnt[(2 * pr) % 6] = xcnt.get((2 * pr) % 6, 0) + 16

    e_r2l = [0] * NTT
    for tt in range(NTT):
        slot = (2 * (tt // 2)) % 6
        nld = 16 * ((tt // 2) // 3 + 1)
        xb = vf(XSL[slot] + 4096 * (tt % 2), 1024)

        @blk.vector
        def _(v, tt=tt, xb=xb, slot=slot, nld=nld):
            if tt == 0:
                W(v, "sm", 112)
            W(v, f"x{slot}", nld)
            v.reduce_sum(vf(R1A + 4 * tt, 1), xb, AX.X).then_inc(sems["dve"], 1)
        bump("dve")

        def _sq(s, tt=tt, xb=xb, slot=slot, nld=nld):
            if tt == 0:
                W(s, "sm", 112)
            W(s, f"x{slot}", nld)
            if tt >= 1:
                W(s, "act", tt)
            s.activation(vf(LSQ2, 1024), xb, AF.Square,
                         accum_out=vf(R2A + 4 * tt, 1)).then_inc(sems["act"], 1)
        SDEF.append(_sq)
        bump("act")
        assert ctr["act"] == tt + 1
        e_r2l[tt] = ctr["dve"]

    e_pb_d = 0

    @blk.vector
    def _(v):
        nonlocal e_pb_d
        W(v, "dve", e_r2l[NTT - 1])
        W(v, "act", NTT)
        step_v(v, lambda: v.tensor_scalar(vf(NBA, 16), vf(R1A, 16), 1.0 / C, None, ALU.mult))
        step_v(v, lambda: v.tensor_scalar(vf(INVA, 16), vf(R2A, 16), 1.0 / C, None, ALU.mult))
        step_v(v, lambda: v.tensor_mul(vf(R1A, 16), vf(NBA, 16), vf(NBA, 16)))
        e_pb_d = step_v(v, lambda: v.tensor_sub(vf(R2A, 16), vf(INVA, 16), vf(R1A, 16)))

    def _stats(s, e_pb_d_v=e_pb_d, ln_ev=ctr["act"] + 1):
        W(s, "dve", e_pb_d_v)
        s.activation(vf(LNT, 16), vf(R2A, 16), AF.Ln, bias=epsc).then_inc(sems["act"], 1)
        W(s, "act", ln_ev)
        s.activation(vf(INVA, 16), vf(LNT, 16), AF.Exp, scale=-0.5).then_inc(sems["act"], 1)
    SDEF.append(_stats)
    bump("act"); bump("act")
    e_pb_a = ctr["act"]

    e_nb_all = 0

    @blk.vector
    def _(v):
        nonlocal e_nb_all
        W(v, "act", e_pb_a)
        step_v(v, lambda: v.tensor_mul(vf(NBA, 16), vf(NBA, 16), vf(INVA, 16)))
        e_nb_all = step_v(v, lambda: v.tensor_scalar(vf(NBA, 16), vf(NBA, 16), -1.0, None, ALU.mult))

    @blk.sync
    def _(sy):
        for tt in range(NTT):
            if tt < 2:
                W(sy, "act", NTT)
                W(sy, "dve", NTT)
            elif tt >= 4:
                W(sy, "act", ev_apply[tt - 4])
            sy.dma_start(out=vf(LXB4[tt % 4], 1024), in_=x_tiles[tt]).then_inc(sems[f"x{tt % 4}"], 16)
    xv = [0] * NTT
    for tt in range(NTT):
        xcnt[tt % 4] = xcnt.get(tt % 4, 0) + 16
        ctr[f"x{tt % 4}"] = xcnt[tt % 4]
        xv[tt] = xcnt[tt % 4]

    @blk.gpsimd
    def _(g):
        W(g, "act", ev_apply[NTT - 1])
        g.dma_start(out=vb(WEOFF + 5 * 16384, 8192), in_=we_d[5]).then_inc(sems["we5"], 16)
    ctr["we5"] = 16

    def emit_z(tt):
        zunits = [0] if tt % 2 == 0 else [2]
        pz = P[0] if tt % 2 == 0 else P[1]
        xf32 = vf(XNF32B[tt % 3], 1024)

        @blk.tensor
        def _(t, tt=tt, pz=pz, xf32=xf32, zunits=zunits):
            W(t, "act", ev_evac32[tt])
            wait_units(t, zunits)
            ins = None
            for ct in range(NCT):
                ins = t.matmul(pz[:, 0:8], xf32[:, ct * 128:(ct + 1) * 128],
                               wg32[:, ct * 8:(ct + 1) * 8], start=(ct == 0), stop=(ct == NCT - 1))
            ins.then_inc(sems["pe"], 1)
        e_z0l[tt] = bump("pe")
        touch(zunits, "pe", e_z0l[tt])

        @blk.vector
        def _(v, tt=tt, pz=pz):
            W(v, "pe", e_z0l[tt])
            v.tensor_copy(zall[:, tt * 8:(tt + 1) * 8], pz[:, 0:8]).then_inc(sems["dve"], 1)
        e_zev[tt] = bump("dve")
        touch(zunits, "dve", e_zev[tt])

    def emit_evac32(tt):
        ph = P[2] if tt % 2 == 0 else P[3]
        units = [4, 5] if tt % 2 == 0 else [6, 7]
        xf32 = vf(XNF32B[tt % 3], 1024)

        def _e32(s, tt=tt, ph=ph, xf32=xf32, e_tr=e_trl[tt],
                 e_z=(e_z0l[tt - 3] if tt >= 3 else 0)):
            W(s, "pe", e_tr)
            if tt >= 3:
                W(s, "pe", e_z)
            s.copy(xf32, ph[:]).then_inc(sems["act"], 1)
        SDEF.append(_e32)
        bump("act")
        assert ctr["act"] == ev_evac32[tt], (ctr["act"], ev_evac32[tt])
        touch(units, "act", ev_evac32[tt])

    for tt in range(NTT):
        xb = vf(LXB4[tt % 4], 1024)
        xn_t = vf(XN32B[tt % 3], 1024)
        units = [4, 5] if tt % 2 == 0 else [6, 7]
        ph = P[2] if tt % 2 == 0 else P[3]

        def _ap(s, tt=tt, xb=xb, xn_t=xn_t, e_nb=e_nb_all, xv_t=xv[tt],
                e_tr=(e_trl[tt - 3] if tt >= 3 else 0)):
            if tt == 0:
                W(s, "dve", e_nb)
            W(s, f"x{tt % 4}", xv_t)
            if tt >= 3:
                W(s, "pe", e_tr)
            s.activation(xn_t, xb, AF.Identity, bias=vf(NBA + 4 * tt, 1),
                         scale=vf(INVA + 4 * tt, 1)).then_inc(sems["act"], 1)
        SDEF.append(_ap)
        bump("act")

        @blk.tensor
        def _(t, tt=tt, xn_t=xn_t, ph=ph, units=units):
            if tt == 0:
                W(t, "sm", 112)
            W(t, "act", ev_apply[tt])
            wait_units(t, units)
            ins = None
            for ct in range(NCT):
                ins = t.transpose(ph[:, ct * 128:(ct + 1) * 128], xn_t[:, ct * 128:(ct + 1) * 128], identf)
            ins.then_inc(sems["pe"], 1)
        e_trl[tt] = bump("pe")
        touch(units, "pe", e_trl[tt], reset=True)

        @blk.vector
        def _(v, tt=tt, ph=ph):
            W(v, "pe", e_trl[tt])
            v.tensor_copy(xnf3[:, :, tt * 128:(tt + 1) * 128],
                          ph[:, 0:1024].rearrange("p (ct t) -> p ct t", t=128)).then_inc(sems["dve"], 1)
        e_xnf[tt] = bump("dve")
        touch(units, "dve", e_xnf[tt])

        if tt >= 1:
            emit_evac32(tt - 1)
        if tt >= 3:
            emit_z(tt - 3)
    emit_evac32(NTT - 1)
    for tt in (NTT - 3, NTT - 2, NTT - 1):
        emit_z(tt)

    e_wtevac = [0] * NTT
    zv = zall.rearrange("p (tt e) -> p tt e", e=8)
    e_gd = 0

    @blk.vector
    def _(v):
        nonlocal e_gd
        W(v, "dve", e_zev[NTT - 1])
        m1b = vf(M1A, 16).unsqueeze(2).broadcast_to([128, 16, 8])
        m2b = vf(M2A, 16).unsqueeze(2).broadcast_to([128, 16, 8])
        i1b = vf(I1A, 16).unsqueeze(2).broadcast_to([128, 16, 8])
        i2b = vf(I2A, 16).unsqueeze(2).broadcast_to([128, 16, 8])
        iotab = iota8.unsqueeze(1).broadcast_to([128, 16, 8])
        ltv = vf(LTA, 128).rearrange("p (tt e) -> p tt e", e=8)
        penv = vf(PENA, 128).rearrange("p (tt e) -> p tt e", e=8)
        eq1v = vf(EQ1A, 128).rearrange("p (tt e) -> p tt e", e=8)
        eq2v = vf(EQ2A, 128).rearrange("p (tt e) -> p tt e", e=8)
        z2v = vf(Z2A, 128).rearrange("p (tt e) -> p tt e", e=8)
        iotr = vf(IOTR, 128).rearrange("p (tt e) -> p tt e", e=8)
        step_v(v, lambda: v.tensor_copy(iotr, iotab))
        step_v(v, lambda: v.tensor_reduce(vf(M1A, 16), zv, AX.X, ALU.max))
        step_v(v, lambda: v.tensor_tensor(ltv, zv, m1b, ALU.is_lt))
        step_v(v, lambda: v.scalar_tensor_tensor(penv, ltv, BIGC, iotr, ALU.mult, ALU.add))
        step_v(v, lambda: v.tensor_reduce(vf(I1A, 16), penv, AX.X, ALU.min))
        step_v(v, lambda: v.tensor_tensor(eq1v, iotr, i1b, ALU.is_equal))
        step_v(v, lambda: v.scalar_tensor_tensor(z2v, eq1v, -BIGC, zall, ALU.mult, ALU.add))
        step_v(v, lambda: v.tensor_reduce(vf(M2A, 16), z2v, AX.X, ALU.max))
        step_v(v, lambda: v.tensor_tensor(ltv, z2v, m2b, ALU.is_lt))
        step_v(v, lambda: v.scalar_tensor_tensor(penv, ltv, BIGC, iotr, ALU.mult, ALU.add))
        step_v(v, lambda: v.tensor_reduce(vf(I2A, 16), penv, AX.X, ALU.min))
        step_v(v, lambda: v.tensor_tensor(eq2v, iotr, i2b, ALU.is_equal))
        e_gd = step_v(v, lambda: v.tensor_sub(vf(DA, 16), vf(M2A, 16), vf(M1A, 16)))

    def _ge(s, e_gd_v=e_gd):
        W(s, "dve", e_gd_v)
        s.activation(vf(W2A, 16), vf(DA, 16), AF.Exp).then_inc(sems["act"], 1)
    SDEF.append(_ge)
    e_ge = bump("act")

    e_w_all = 0

    @blk.vector
    def _(v):
        nonlocal e_w_all
        W(v, "act", e_ge)
        w1b = vf(W1A, 16).unsqueeze(2).broadcast_to([128, 16, 8])
        w2b = vf(W2A, 16).unsqueeze(2).broadcast_to([128, 16, 8])
        eq1v = vf(EQ1A, 128).rearrange("p (tt e) -> p tt e", e=8)
        eq2v = vf(EQ2A, 128).rearrange("p (tt e) -> p tt e", e=8)
        ltv = vf(LTA, 128).rearrange("p (tt e) -> p tt e", e=8)
        wtv = vf(WTA, 128).rearrange("p (tt e) -> p tt e", e=8)
        step_v(v, lambda: v.tensor_scalar(vf(W1A, 16), vf(W2A, 16), 1.0, None, ALU.add))
        step_v(v, lambda: v.reciprocal(vf(W2A, 16), vf(W1A, 16)))
        step_v(v, lambda: v.tensor_scalar(vf(W1A, 16), vf(W2A, 16), -1.0, 1.0, ALU.mult, ALU.add))
        step_v(v, lambda: v.tensor_tensor(wtv, eq1v, w2b, ALU.mult))
        step_v(v, lambda: v.tensor_tensor(ltv, eq2v, w1b, ALU.mult))
        step_v(v, lambda: v.tensor_add(wall.rearrange("p (tt e) -> p tt e", e=8), wtv, ltv))
        e_w_all = step_v(v, lambda: v.tensor_copy(vb(WBFA, 128), wall))

    for tt in range(NTT):
        units = [6] if tt % 2 == 0 else [4]
        pz = P[3] if tt % 2 == 0 else P[2]
        pzb = pz[:].bitcast(BF16)

        @blk.tensor
        def _(t, tt=tt, pzb=pzb, units=units):
            W(t, "dve", e_w_all)
            wait_units(t, units)
            t.transpose(pzb[0:8, 32:160], vb(WBFA, 128)[:, tt * 8:(tt + 1) * 8], ident).then_inc(sems["pe"], 1)
        e_wtr = bump("pe")
        touch(units, "pe", e_wtr)

        @blk.vector
        def _(v, tt=tt, pzb=pzb):
            W(v, "pe", e_wtr)
            v.tensor_copy(wt_sb[:, tt * 128:(tt + 1) * 128], pzb[0:8, 32:160]).then_inc(sems["dve"], 1)
        e_wtevac[tt] = bump("dve")
        touch(units, "dve", e_wtevac[tt])

    e_wtr_all = ctr["pe"]

    @blk.gpsimd
    def _(g):
        W(g, "act", ev_evac32[NTT - 1])
        W(g, "pe", e_z0l[NTT - 1])
        W(g, "dve", e_w_all)
        W(g, "pe", e_wtr_all)
        for e in (6, 7):
            g.dma_start(out=vb(WEOFF + e * 16384, 8192), in_=we_d[e]).then_inc(sems[f"we{e}"], 16)
    ctr["we6"] = 16; ctr["we7"] = 16

    if PHASES <= 2:
        return finish_trunc()

    e_silu = [0] * NTT
    xne_guard = [0, 0, 0]
    e_xne_last = [0] * NTT
    e_wbc_l = [0] * NTT
    XNE3 = [WBC, XNE[0], XNE[1]]

    def emit_wbc(tt):
        pwbc = P[2] if tt % 2 == 0 else P[3]
        wbc_units = [4, 5] if tt % 2 == 0 else [6, 7]

        @blk.tensor
        def _(t, tt=tt, pwbc=pwbc, wbc_units=wbc_units):
            W(t, "dve", e_wtevac[tt])
            if tt >= 2:
                W(t, "dve", e_xne_last[tt - 2])
            wait_units(t, wbc_units)
            ins = None
            for e in range(E):
                ins = t.matmul(pwbc[:, e * 128:(e + 1) * 128], sel8[:, e * 128:(e + 1) * 128],
                               wt_sb[:, tt * 128:(tt + 1) * 128], start=True, stop=True)
            ins.then_inc(sems["pe"], 1)
        e_wbc_l[tt] = bump("pe")
        touch(wbc_units, "pe", e_wbc_l[tt])

    emit_wbc(0)
    for tt in range(NTT):
        accP = P[tt % 2]
        acc_units = [0, 1] if tt % 2 == 0 else [2, 3]
        pwbc = P[2] if tt % 2 == 0 else P[3]
        wbc_units = [4, 5] if tt % 2 == 0 else [6, 7]

        for e in range(E):
            @blk.vector
            def _(v, tt=tt, e=e, pwbc=pwbc):
                W(v, "pe", max(e_wbc_l[tt], xne_guard[e % 3]))
                wbc_e = pwbc[:].rearrange("p (e t) -> p e t", t=128)[:, e, :]
                v.tensor_tensor(vb(XNE3[e % 3], 1024).rearrange("p (ct t) -> p ct t", t=128),
                                xnf3[:, :, tt * 128:(tt + 1) * 128],
                                wbc_e.unsqueeze(1).broadcast_to([128, 8, 128]),
                                ALU.mult).then_inc(sems["dve"], 1)
            e_xne = bump("dve")
            if e == E - 1:
                e_xne_last[tt] = e_xne
                touch(wbc_units, "dve", e_xne)

            @blk.tensor
            def _(t, tt=tt, e=e, e_xne=e_xne):
                W(t, "dve", e_xne)
                W(t, f"we{e}", 16)
                if e == 0:
                    wait_units(t, acc_units)
                we_v = vb(WEOFF + e * 16384, 8192)
                ins = None
                for ct in range(NCT):
                    for hf in range(2):
                        ins = t.matmul(accP[:, hf * 512:(hf + 1) * 512],
                                       vb(XNE3[e % 3], 1024)[:, ct * 128:(ct + 1) * 128],
                                       we_v[:, ct * 1024 + hf * 512: ct * 1024 + (hf + 1) * 512],
                                       start=(e == 0 and ct == 0), stop=False)
                ins.then_inc(sems["pe"], 1)
            xne_guard[e % 3] = bump("pe")
            if e == 0 and tt + 1 < NTT:
                emit_wbc(tt + 1)

        @blk.tensor
        def _(t, tt=tt):
            if tt == 0:
                W(t, "bes", 16)
            ins = None
            for hf in range(2):
                ins = t.matmul(accP[:, hf * 512:(hf + 1) * 512], wt_sb[:, tt * 128:(tt + 1) * 128],
                               be_sb[:, hf * 512:(hf + 1) * 512], start=False, stop=True)
            ins.then_inc(sems["pe"], 1)
        e_acc = bump("pe")
        touch(acc_units, "pe", e_acc)

        def _si(s, tt=tt, accP=accP, e_acc=e_acc):
            W(s, "pe", e_acc)
            s.activation(gsb[:, tt * 1024:(tt + 1) * 1024], accP[:], AF.Silu).then_inc(sems["act"], 1)
        SDEF.append(_si)
        e_silu[tt] = bump("act")
        touch(acc_units, "act", e_silu[tt])

    e_ffn_all = ctr["pe"]

    if PHASES <= 3:
        return finish_trunc()

    @blk.vector
    def _(v):
        W(v, "sm", 112)
        v.tensor_scalar(vf(BQ8, 8), bqpp[:, 0:8], SCALE, None, ALU.mult).then_inc(sems["dve"], 1)
    e_bq8 = bump("dve")

    qsb3 = vb(QOFF, 16384).rearrange("p (ot t) -> p ot t", t=T)
    ksb3 = vb(KOFF, 16384).rearrange("p (ot t) -> p ot t", t=T)
    qsb = vb(QOFF, 16384)
    ksb = vb(KOFF, 16384)
    vsb = vb(VOFF, 16384)
    vsb3 = vb(VOFF, 16384).rearrange("p (tt o) -> p tt o", o=1024)
    wpt = vb(WPT, 8192)
    e_grp = [0] * 6
    slot_i = [0]

    def next_slot():
        u = slot_i[0] % 8
        slot_i[0] += 1
        return u

    for g4 in range(6):
        wq_v = vb(WQT[g4 % 2], 4096)
        if g4 < 4:
            for oi in range(4):
                ot = g4 * 4 + oi
                for chunk in range(4):
                    u = next_slot()
                    ps = P[u // 2][:, (u % 2) * 512:(u % 2) * 512 + 512]

                    @blk.tensor
                    def _(t, g4=g4, oi=oi, chunk=chunk, u=u, ps=ps, wq_v=wq_v):
                        W(t, f"wq{g4 % 2}", ev_wq[g4])
                        wait_units(t, [u])
                        ins = None
                        for ct in range(NCT):
                            ins = t.matmul(ps, wq_v[:, ct * 512 + oi * 128: ct * 512 + (oi + 1) * 128],
                                           xnf3[:, ct, chunk * 512:(chunk + 1) * 512],
                                           start=(ct == 0), stop=(ct == NCT - 1))
                        ins.then_inc(sems["pe"], 1)
                    e_mm = bump("pe")
                    touch([u], "pe", e_mm)

                    if ot < 8:
                        def _qe(s, ot=ot, chunk=chunk, ps=ps, e_mm=e_mm, e_bq8=e_bq8):
                            W(s, "pe", e_mm)
                            if ot == 0 and chunk == 0:
                                W(s, "dve", e_bq8)
                            s.activation(qsb[:, ot * 2048 + chunk * 512: ot * 2048 + (chunk + 1) * 512],
                                         ps, AF.Identity, bias=vf(BQ8 + 4 * ot, 1), scale=SCALE).then_inc(sems["act"], 1)
                        SDEF.append(_qe)
                        touch([u], "act", bump("act"))
                    else:
                        @blk.vector
                        def _(v, ot=ot, chunk=chunk, ps=ps, e_mm=e_mm):
                            W(v, "pe", e_mm)
                            v.tensor_scalar(ksb[:, (ot - 8) * 2048 + chunk * 512:
                                                (ot - 8) * 2048 + (chunk + 1) * 512],
                                            ps, bqpp[:, ot:ot + 1], None, ALU.add).then_inc(sems["dve"], 1)
                        touch([u], "dve", bump("dve"))
        else:
            vh = g4 - 4
            for tt in range(NTT):
                u = next_slot()
                ps = P[u // 2][:, (u % 2) * 512:(u % 2) * 512 + 512]

                @blk.tensor
                def _(t, g4=g4, tt=tt, vh=vh, u=u, ps=ps, wq_v=wq_v):
                    W(t, f"wq{g4 % 2}", ev_wq[g4])
                    if tt == 0:
                        W(t, "vr", 32)
                    wait_units(t, [u])
                    for ct in range(NCT):
                        t.matmul(ps, xnf3[:, ct, tt * 128:(tt + 1) * 128],
                                 wq_v[:, ct * 512:(ct + 1) * 512], start=(ct == 0), stop=False)
                    t.matmul(ps, ones_row, vrow[:, vh * 512:(vh + 1) * 512],
                             start=False, stop=True).then_inc(sems["pe"], 1)
                e_mm = bump("pe")
                touch([u], "pe", e_mm)

                @blk.vector
                def _(v, tt=tt, vh=vh, ps=ps, e_mm=e_mm):
                    W(v, "pe", e_mm)
                    v.tensor_copy(vsb[:, tt * 1024 + vh * 512: tt * 1024 + (vh + 1) * 512],
                                  ps).then_inc(sems["dve"], 1)
                touch([u], "dve", bump("dve"))
        e_grp[g4] = ctr["pe"]

    e_qkv_pe = ctr["pe"]
    e_qkv_dve = ctr["dve"]
    e_qkv_act = ctr["act"]

    @blk.gpsimd
    def _(g):
        W(g, "pe", e_ffn_all)
        for g4 in range(2):
            g.dma_start(out=vb(WQT[g4 % 2], 4096).rearrange("p (ct o) -> p ct o", o=512),
                        in_=wq_r[:, :, g4 * 512:(g4 + 1) * 512]).then_inc(sems[f"wq{g4 % 2}"], 16)
        g.dma_start(out=wpt.rearrange("p (ct o) -> p ct o", o=1024),
                    in_=wp_r[:, :, :]).then_inc(sems["wp"], 16)
        g.dma_start(out=vrow, in_=vrow_d[:, :]).then_inc(sems["vr"], 16)
        g.dma_start(out=prow, in_=prow_d[:, :]).then_inc(sems["vr"], 16)
        for g4 in range(2, 6):
            W(g, "pe", e_grp[g4 - 2])
            g.dma_start(out=vb(WQT[g4 % 2], 4096).rearrange("p (ct o) -> p ct o", o=512),
                        in_=wq_r[:, :, g4 * 512:(g4 + 1) * 512]).then_inc(sems[f"wq{g4 % 2}"], 16)
    ctr["wq0"] = 48; ctr["wq1"] = 48; ctr["wp"] = 16
    ctr["vr"] = 32

    if PHASES <= 4:
        return finish_trunc()

    NJ = 4 * NTT
    p1b = P[1][:].bitcast(BF16)
    p2b = P[2][:].bitcast(BF16)
    aun4 = [vb(AUN4[i], 512) for i in range(4)]
    amskv = vb(AMSK, 512)
    atb2 = [vb(ATB2[0], 512), vb(ATB2[1], 512), vb(ATB2[2], 512)]
    attn4 = vb(ATTN, 2048).rearrange("p (c e t) -> p c e t", c=2, e=8)
    t1b = vb(XNE[1], 1024)

    def spsum(j):
        r = j % 3
        if r == 0:
            return P[0][:, 0:512]
        if r == 1:
            return P[0][:, 512:1024]
        return P[1][:, 0:512]

    def atpsum(j):
        if j % 2 == 0:
            return p1b[:, 1024:2048]
        return p2b[:, 0:1024]

    def avpsum(j):
        return P[3][:, (j % 2) * 512:(j % 2) * 512 + 512]

    prpsum = P[2][:, 512:1024]
    e_s = [0] * NJ
    e_exp = [0] * NJ
    e_norm = [0] * NJ
    e_atr = [0] * NJ
    e_atev = [0] * NJ
    e_av = [0] * NJ
    e_avev = [0] * NJ
    e_projA = [0] * NTT
    e_projB = [0] * NTT
    e_prc1 = [0] * NTT
    e_prc2 = [0] * NTT
    e_cmb1 = [0] * NTT
    e_cmb2 = [0] * NTT
    e_cmb = [0] * NTT
    e_xrs = [0] * NTT

    @blk.sync
    def _(sy):
        W(sy, "pe", e_qkv_pe)
        sy.dma_start(out=vf(XB[0], 1024), in_=x_tiles[0]).then_inc(sems["xa"], 16)
        sy.dma_start(out=vf(XB[1], 1024), in_=x_tiles[1]).then_inc(sems["xb"], 16)
    ctr["xa"] = 16; ctr["xb"] = 16
    e_xrs[0] = 16; e_xrs[1] = 16

    for it in range(NJ + 14):
        js = it
        jm = it - 1
        jn = it - 2
        ja = it - 4
        jv = it - 6
        b_prA = (it - 11) // 4 if (it - 11) >= 0 and (it - 11) % 4 == 0 else -1
        b_cmA = (it - 12) // 4 if (it - 12) >= 0 and (it - 12) % 4 == 0 else -1
        b_prB = (it - 13) // 4 if (it - 13) >= 0 and (it - 13) % 4 == 0 else -1
        b_cmB = (it - 14) // 4 if (it - 14) >= 0 and (it - 14) % 4 == 0 else -1

        @blk.tensor
        def _(t, it=it, js=js, ja=ja, jv=jv, b_prA=b_prA, b_prB=b_prB):
            if 0 <= ja < NJ:
                W(t, "dve", e_norm[ja])
                if ja >= 2:
                    W(t, "act", e_atev[ja - 2])
                ap = atpsum(ja)
                for k in range(4):
                    ins = t.transpose(ap[:, k * 128:(k + 1) * 128],
                                      aun4[ja % 4][:, k * 128:(k + 1) * 128], ident)
                ins.then_inc(sems["pe"], 1)
                e_atr[ja] = bump("pe")
            if 0 <= jv < NJ:
                b, qa = jv // 4, jv % 4
                W(t, "act", e_atev[jv])
                if jv >= 2:
                    W(t, "act", e_avev[jv - 2])
                avp = avpsum(jv)
                for k in range(4):
                    h = qa * 4 + k
                    ins = t.matmul(avp[:, k * 128:(k + 1) * 128],
                                   vsb3[:, b, (h // 2) * 128:(h // 2 + 1) * 128],
                                   atb2[jv % 3][:, k * 128:(k + 1) * 128], start=True, stop=True)
                ins.then_inc(sems["pe"], 1)
                e_av[jv] = bump("pe")
            for b_p, hf in ((b_prA, 0), (b_prB, 1)):
                if b_p >= 0 and b_p < NTT:
                    b = b_p
                    if hf == 0:
                        W(t, "act", e_avev[4 * b + 3])
                        if b == 0:
                            W(t, "wp", 16)
                    W(t, "act", e_prc1[b] if hf == 1 else (e_prc2[b - 1] if b >= 1 else 0))
                    for ct in range(NCT):
                        t.matmul(prpsum, attn4[:, b % 2, ct, :],
                                 wpt[:, ct * 1024 + hf * 512: ct * 1024 + (hf + 1) * 512],
                                 start=(ct == 0), stop=False)
                    ins = t.matmul(prpsum, ones_row,
                                   prow[:, hf * 512:(hf + 1) * 512], start=False, stop=True)
                    ins.then_inc(sems["pe"], 1)
                    if hf == 0:
                        e_projA[b] = bump("pe")
                    else:
                        e_projB[b] = bump("pe")
            if js < NJ:
                b, qa = js // 4, js % 4
                if js == 0:
                    W(t, "dve", e_qkv_dve)
                    W(t, "act", e_qkv_act)
                if js >= 3:
                    W(t, "act", e_exp[js - 3])
                sp = spsum(js)
                for k in [0, 2]:
                    h = qa * 4 + k
                    t.matmul(sp[:, k * 128:(k + 1) * 128],
                             qsb3[0:64, h // 2, b * 128:(b + 1) * 128],
                             ksb3[0:64, h // 2, b * 128:(b + 1) * 128], start=True, stop=True)
                t.ldweights(ident[:])
                for k in [1, 3]:
                    h = qa * 4 + k
                    ins = t.matmul(sp[:, k * 128:(k + 1) * 128],
                                   qsb3[64:128, h // 2, b * 128:(b + 1) * 128],
                                   ksb3[64:128, h // 2, b * 128:(b + 1) * 128], start=True, stop=True)
                ins.then_inc(sems["pe"], 1)
                e_s[js] = bump("pe")

        if 0 <= jm < NJ:
            @blk.vector
            def _(v, jm=jm):
                W(v, "pe", e_s[jm])
                if jm >= 2:
                    W(v, "act", e_exp[jm - 2])
                v.reduce_max(vf(MNEG2[jm % 2], 1), spsum(jm), AX.X, negate=True).then_inc(sems["dve"], 1)
            e_mx = bump("dve")

        if 0 <= jm < NJ:
            e_exp[jm] = bump("act")

            def _exp(s, jm=jm, e_mx=ctr["dve"],
                     e_atr_v=(e_atr[jm - 4] if 4 <= jm < NJ else 0)):
                W(s, "dve", e_mx)
                if jm >= 4:
                    W(s, "pe", e_atr_v)
                s.activation(aun4[jm % 4], spsum(jm), AF.Exp,
                             bias=vf(MNEG2[jm % 2], 1)).then_inc(sems["act"], 1)
            SDEF.append(_exp)

        if 0 <= ja < NJ:
            e_atev[ja] = bump("act")

            def _atev(s, ja=ja, e_atr_v=e_atr[ja],
                      e_av_w=(e_av[ja - 3] if ja >= 3 else 0)):
                W(s, "pe", e_atr_v)
                if ja >= 3:
                    W(s, "pe", e_av_w)
                s.copy(atb2[ja % 3], atpsum(ja)[:, 0:512]).then_inc(sems["act"], 1)
            SDEF.append(_atev)

        if 0 <= jv < NJ:
            bump("act")
            e_avev[jv] = bump("act")

            def _avev(s, jv=jv, e_av_v=e_av[jv],
                      e_pB=(e_projB[jv // 4 - 2] if jv // 4 >= 2 and jv % 4 == 0 else 0)):
                b, qa = jv // 4, jv % 4
                W(s, "pe", e_av_v)
                if b >= 2 and qa == 0:
                    W(s, "pe", e_pB)
                p2v = avpsum(jv).rearrange("p (e two t) -> p e two t", two=2, t=128)
                ct0 = qa * 2
                s.copy(attn4[0:64, b % 2, ct0:ct0 + 2, :], p2v[0:64, :, 0, :]).then_inc(sems["act"], 1)
                s.copy(attn4[64:128, b % 2, ct0:ct0 + 2, :], p2v[64:128, :, 1, :]).then_inc(sems["act"], 1)
            SDEF.append(_avev)

        for b_p, hf in ((b_prA, 0), (b_prB, 1)):
            if 0 <= b_p < NTT:
                ev = bump("act")
                (e_prc1 if hf == 0 else e_prc2)[b_p] = ev
                def _prc(s, hf=hf,
                         e_pr=(e_projA[b_p] if hf == 0 else e_projB[b_p]),
                         e_war=((e_cmb1[b_p - 1] if hf == 0 else e_cmb2[b_p - 1]) if b_p >= 1 else 0)):
                    W(s, "pe", e_pr)
                    if e_war:
                        W(s, "dve", e_war)
                    s.copy(vf(PRC[hf], 512), prpsum).then_inc(sems["act"], 1)
                SDEF.append(_prc)

        @blk.vector
        def _(v, it=it, jn=jn, ja=ja, b_cmA=b_cmA, b_cmB=b_cmB):
            if 0 <= jn < NJ:
                W(v, "act", e_exp[jn])
                if jn >= 1:
                    W(v, "dve", e_norm[jn - 1])
                v.tensor_tensor(amskv.rearrange("p (e t) -> p e t", t=128),
                                aun4[jn % 4].rearrange("p (e t) -> p e t", t=128),
                                b01.unsqueeze(1).broadcast_to([128, 4, 128]), ALU.mult).then_inc(sems["dve"], 1)
                val = bump("dve"); W(v, "dve", val)
                v.tensor_reduce(vf(R8N, 4), amskv.rearrange("p (e t) -> p e t", t=128),
                                AX.X, ALU.add).then_inc(sems["dve"], 1)
                val = bump("dve"); W(v, "dve", val)
                v.reciprocal(vf(RHON, 4), vf(R8N, 4)).then_inc(sems["dve"], 1)
                val = bump("dve"); W(v, "dve", val)
                ins = None
                for k in range(4):
                    ins = v.tensor_scalar(aun4[jn % 4][:, k * 128:(k + 1) * 128],
                                          amskv[:, k * 128:(k + 1) * 128],
                                          vf(RHON + 4 * k, 1), None, ALU.mult)
                ins.then_inc(sems["dve"], 1)
                e_norm[jn] = bump("dve")
            for b_c, hf in ((b_cmA, 0), (b_cmB, 1)):
                if b_c >= 0 and b_c < NTT:
                    b = b_c
                    W(v, "act", e_prc1[b] if hf == 0 else e_prc2[b])
                    if hf == 0:
                        W(v, "xa" if b % 2 == 0 else "xb", e_xrs[b])
                        if b >= 2:
                            W(v, "so0" if b % 2 == 0 else "so1", 16 * ((b - 2) // 2 + 1))
                        W(v, "act", e_silu[b])
                    v.tensor_tensor(t1b[:, hf * 512:(hf + 1) * 512], vf(PRC[hf], 512),
                                    gsb[:, b * 1024 + hf * 512: b * 1024 + (hf + 1) * 512],
                                    ALU.mult).then_inc(sems["dve"], 1)
                    if hf == 0:
                        e_cmb1[b] = bump("dve")
                    else:
                        e_cmb2[b] = bump("dve")
                    val = ctr["dve"]; W(v, "dve", val)
                    v.tensor_add(vf(OBUF[b % 2] + 2048 * hf, 512),
                                 t1b[:, hf * 512:(hf + 1) * 512],
                                 vf(XB[b % 2] + 2048 * hf, 512)).then_inc(sems["dve"], 1)
                    if hf == 1:
                        e_cmb[b] = bump("dve")
                    else:
                        bump("dve")

        if b_cmB >= 0 and b_cmB < NTT:
            @blk.sync
            def _(sy, b=b_cmB):
                W(sy, "dve", e_cmb[b])
                so_s = "so0" if b % 2 == 0 else "so1"
                sy.dma_start(out=out_tiles[b], in_=vf(OBUF[b % 2], 1024)).then_inc(sems[so_s], 16)
                bump(so_s, 16)
                if b + 2 < NTT:
                    xs = "xa" if b % 2 == 0 else "xb"
                    sy.dma_start(out=vf(XB[b % 2], 1024), in_=x_tiles[b + 2]).then_inc(sems[xs], 16)
                    e_xrs[b + 2] = bump(xs, 16)

    s_flush()

    @blk.sync
    def _(sy):
        W(sy, "so0", ctr["so0"])
        W(sy, "so1", ctr["so1"])
        W(sy, "pe", ctr["pe"]); W(sy, "dve", ctr["dve"]); W(sy, "act", ctr["act"])

    es.close()
    return nc


def _host_inputs(Wqkv, bqkv, Wproj, bproj, Wg, We, be):
    bf = ml_dtypes.bfloat16
    sel = np.zeros((8, 1024), np.float32)
    for e in range(8):
        sel[e, e * 128:(e + 1) * 128] = 1.0
    return {
        "wqkvT": np.ascontiguousarray(Wqkv.T),
        "bqpp": np.ascontiguousarray(bqkv.reshape(24, 128).T),
        "vrow": np.ascontiguousarray(bqkv[None, 2048:]),
        "wprojT": np.ascontiguousarray(Wproj.T),
        "prow": np.ascontiguousarray(bproj[None, :]),
        "wg": np.ascontiguousarray(Wg.reshape(8, 128, 8).transpose(1, 0, 2).reshape(128, 64)),
        "we": np.ascontiguousarray(We.reshape(8, 8, 128, 1024).transpose(0, 2, 1, 3).reshape(8, 128, 8192)),
        "be": np.ascontiguousarray(be),
        "ident": np.eye(128, dtype=bf),
        "identf": np.eye(128, dtype=np.float32),
        "b01": np.kron(np.eye(32, dtype=np.float32), np.ones((4, 4), np.float32)).astype(bf),
        "sel8": sel.astype(bf),
        "iota": np.tile(np.concatenate([np.arange(8, dtype=np.float32), [EPS]]), (128, 1)),
    }


def kernel(L0_attn_x, L0_attn_Wqkv, L0_attn_bqkv, L0_attn_Wproj, L0_attn_bproj,
           L0_ffn_Wg, L0_ffn_We, L0_ffn_be):
    x = np.asarray(L0_attn_x, np.float32)
    shared = _host_inputs(np.asarray(L0_attn_Wqkv, np.float32), np.asarray(L0_attn_bqkv, np.float32),
                          np.asarray(L0_attn_Wproj, np.float32), np.asarray(L0_attn_bproj, np.float32),
                          np.asarray(L0_ffn_Wg, np.float32), np.asarray(L0_ffn_We, np.float32),
                          np.asarray(L0_ffn_be, np.float32))
    nc = build_nc()
    in_maps = []
    for k in range(NCORES):
        m = dict(shared)
        m["x"] = np.ascontiguousarray(x[:, k * 512:(k + 1) * 512, :].transpose(1, 0, 2).reshape(T, C))
        in_maps.append(m)
    res = run_bass_kernel_spmd(nc, in_maps, core_ids=list(range(NCORES)))
    outs = [np.asarray(res.results[k]["out"]).reshape(512, 4, C).transpose(1, 0, 2)
            for k in range(NCORES)]
    return np.ascontiguousarray(np.concatenate(outs, axis=1))


# version 19
# speedup vs baseline: 1.0320x; 1.0320x over previous
import numpy as np
import ml_dtypes
import concourse.bass as bass
import concourse.mybir as mybir
from concourse.bass_utils import run_bass_kernel_spmd
from contextlib import ExitStack

F32 = mybir.dt.float32
BF16 = mybir.dt.bfloat16
AF = mybir.ActivationFunctionType
ALU = mybir.AluOpType
AX = mybir.AxisListType

NCORES = 8
T = 2048
C = 1024
NTT = 16
NCT = 8
E = 8
EPS = 1e-5
SCALE = 0.125
BIGC = 1.0e9
PHASES = 5
NBLK_RUN = 16

XN_F = 0
AUN4 = [0, 1024, 2048, 3072]
AMSK = 4096
ATB2 = [5120, 6144, 27648]
PRC = [28672, 30720]
LXB4 = [163840, 167936, 147456, 151552]
ATTN = 7168
OBUF = [11264, 15360]
XB = [19456, 23552]
G_SB = 32768
WEOFF = 65536
LXB = [163840, 167936, 172032]
XSL = [163840, 167936, 172032, 176128, 180224, 184320]
LSQ2 = 192512
XN32B = [172032, 176128, 180224]
XNF32B = [184320, 188416, 192512]
GV = 163840
M1A = GV; M2A = GV + 64; I1A = GV + 128; I2A = GV + 192; DA = GV + 256
W2A = GV + 320; W1A = GV + 384
LTA = GV + 448; PENA = GV + 960; EQ1A = GV + 1472; EQ2A = GV + 1984
Z2A = GV + 2496; WTA = GV + 3008
IOTR = GV + 3520; WBFA = GV + 4032
LSQ = 176128
XN32 = [180224, 184320]
XNF32 = [188416, 192512, 176128]
QOFF = 65536
KOFF = 98304
VOFF = 131072
WQT = [163840, 172032]
WPT = 180224
WBC = 196608
XNE = [198656, 200704]
IDENTF = 198656
WG32 = 199168
ZALL = 199424
R1A = 199936; R2A = 200000; INVA = 200064; NBA = 200128; LNT = 200192
SM0 = 202752
IDENT = SM0
B01 = SM0 + 256
SEL8 = SM0 + 512
IOTA = SM0 + 2560
WGOFF = SM0 + 2600
BQPP = SM0 + 2728
BE = SM0 + 2824
WT = SM0 + 4872
VROW = WT
PROW = WT + 2048
WALL = SM0 + 8968
WBF = SM0 + 9480
ZB = SM0 + 9496
GS = SM0 + 9528
M1 = GS; I1 = GS + 4; M2 = GS + 8; I2 = GS + 12; DD = GS + 16; W2V = GS + 20; W1V = GS + 24
LT = GS + 32; PEN = GS + 64; EQ1 = GS + 96; EQ2 = GS + 128; Z2 = GS + 160; WTMP = GS + 192
BQ8 = GS
MNEG2 = [GS + 32, GS + 36]
R8N = GS + 40
RHON = GS + 72
MST = GS + 224
R1S = MST; R2S = MST + 4; MEAN = MST + 8; VAR0 = MST + 12; MSQ = MST + 16
VAR = MST + 20; LNV = MST + 24; INV = MST + 28; NB = MST + 32
AST = MST + 36
MNEG = AST; BIASC = AST + 4; R8 = AST + 8; RHO = AST + 40
ARENA_END = AST + 72
ARENA_BYTES = 212736
assert ARENA_END <= ARENA_BYTES


def build_nc():
    nc = bass.Bass("TRN2", target_bir_lowering=False)
    di = {}

    def dram_in(name, shape, dt=F32):
        di[name] = nc.dram_tensor(name, shape, dt, kind="ExternalInput").ap()
        return di[name]

    x_d = dram_in("x", [T, C])
    wqkvT_d = dram_in("wqkvT", [C, 3 * C])
    bqpp_d = dram_in("bqpp", [128, 24])
    vrow_d = dram_in("vrow", [1, C])
    wprojT_d = dram_in("wprojT", [C, C])
    prow_d = dram_in("prow", [1, C])
    wg_d = dram_in("wg", [128, 64])
    we_d = dram_in("we", [E, 128, 8192])
    be_d = dram_in("be", [E, C])
    ident_d = dram_in("ident", [128, 128], BF16)
    identf_d = dram_in("identf", [128, 128])
    b01_d = dram_in("b01", [128, 128], BF16)
    sel8_d = dram_in("sel8", [8, 1024], BF16)
    iota_d = dram_in("iota", [128, 9])
    out_d = nc.dram_tensor("out", [T, C], F32, kind="ExternalOutput").ap()

    x_tiles = x_d.rearrange("(n p) c -> n p c", p=128)
    out_tiles = out_d.rearrange("(n p) c -> n p c", p=128)
    wq_r = wqkvT_d.rearrange("(ct p) o -> p ct o", p=128)
    wp_r = wprojT_d.rearrange("(ct p) o -> p ct o", p=128)

    es = ExitStack()
    ar_h = es.enter_context(nc.sbuf_tensor("arena", [128, ARENA_BYTES], mybir.dt.uint8))
    ar = ar_h[:]

    def vf(off, n, p0=0, p1=128):
        return ar[p0:p1, off:off + 4 * n].bitcast(F32)

    def vb(off, n, p0=0, p1=128):
        return ar[p0:p1, off:off + 2 * n].bitcast(BF16)

    xnf3 = vb(XN_F, 16384).rearrange("p (ct t) -> p ct t", t=T)
    gsb = vb(G_SB, 16384)
    ident = vb(IDENT, 128)
    identf = vf(IDENTF, 128)
    wg32 = vf(WG32, 64)
    zall = vf(ZALL, 128)
    b01 = vb(B01, 128)
    sel8 = vb(SEL8, 1024, 0, 8)
    ones_row = sel8[0:1, 0:128]
    iota9 = vf(IOTA, 9)
    iota8 = iota9[:, 0:8]
    epsc = iota9[:, 8:9]
    bqpp = vf(BQPP, 24)
    be_sb = vb(BE, 1024, 0, 8)
    wt_sb = vb(WT, 2048, 0, 8)
    vrow = vb(VROW, 1024, 0, 1)
    prow = vb(PROW, 1024, 0, 1)
    wall = vf(WALL, 128)
    wbf = vb(WBF, 8)
    zb = vf(ZB, 8)

    P = [es.enter_context(nc.psum_tensor(f"P{i}", [128, 1024], F32)) for i in range(4)]

    sems = {}
    for name in (["pe", "dve", "act", "sm", "x0", "x1", "x2", "x3", "x4", "x5", "xa", "xb",
                  "wg", "bes", "vr", "wq0", "wq1", "wp", "so0", "so1"]
                 + [f"we{e}" for e in range(8)]):
        sems[name] = es.enter_context(nc.semaphore(name))
    ctr = {k: 0 for k in sems}
    punit = {u: {} for u in range(8)}

    def bump(name, amt=1):
        ctr[name] += amt
        return ctr[name]

    def W(eng, name, val):
        if val and val > 0:
            eng.wait_ge(sems[name], val)

    def wait_units(eng, units):
        need = {}
        for u in units:
            for nm, v in punit[u].items():
                need[nm] = max(need.get(nm, 0), v)
        for nm, v in need.items():
            W(eng, nm, v)

    def touch(units, eng, val, reset=False):
        for u in units:
            if reset:
                punit[u] = {}
            punit[u][eng] = max(punit[u].get(eng, 0), val)

    blk = es.enter_context(nc.Block())

    SDEF = []

    def s_flush():
        if SDEF:
            @blk.scalar
            def _(s):
                for fn in SDEF:
                    fn(s)
            SDEF.clear()

    def finish_trunc():
        s_flush()

        @blk.sync
        def _(sy):
            for nm in sems:
                W(sy, nm, ctr[nm])
        es.close()
        return nc

    ev_wq = [16 * (g // 2 + 1) for g in range(6)]

    @blk.sync
    def _(sy):
        sy.dma_start(out=ident, in_=ident_d[:, :]).then_inc(sems["sm"], 16)
        sy.dma_start(out=b01, in_=b01_d[:, :]).then_inc(sems["sm"], 16)
        sy.dma_start(out=sel8, in_=sel8_d[:, :]).then_inc(sems["sm"], 16)
        sy.dma_start(out=iota9, in_=iota_d[:, :]).then_inc(sems["sm"], 16)
        sy.dma_start(out=bqpp, in_=bqpp_d[:, :]).then_inc(sems["sm"], 16)
        sy.dma_start(out=identf, in_=identf_d[:, :]).then_inc(sems["sm"], 16)
        sy.dma_start(out=wg32, in_=wg_d[:, :]).then_inc(sems["sm"], 16)
    ctr["sm"] = 112

    ev_apply = [NTT + 3] + [NTT + 2 + 2 * tt for tt in range(1, NTT)]
    ev_evac32 = [NTT + 5 + 2 * tt for tt in range(NTT - 1)] + [NTT + 2 + 2 * NTT]

    @blk.gpsimd
    def _(g):
        g.dma_start(out=be_sb, in_=be_d[:, :]).then_inc(sems["bes"], 16)
        W(g, "act", NTT)
        for e in range(5):
            g.dma_start(out=vb(WEOFF + e * 16384, 8192), in_=we_d[e]).then_inc(sems[f"we{e}"], 16)
    ctr["bes"] = 16
    for e in range(5):
        ctr[f"we{e}"] = 16

    def step_v(v, fn):
        fn().then_inc(sems["dve"], 1)
        val = bump("dve")
        W(v, "dve", val)
        return val

    e_xnf = [0] * NTT
    e_trl = [0] * NTT
    e_z0l = [0] * NTT
    e_zev = [0] * NTT
    xcnt = {k: 0 for k in range(6)}

    x_pairs = x_d.rearrange("(a p) c -> p a c", p=128)

    @blk.sync
    def _(sy):
        for pr in range(NTT // 2):
            tt = 2 * pr
            if tt >= 6:
                W(sy, "act", tt - 6 + 2)
                W(sy, "dve", tt - 6 + 2)
            sy.dma_start(out=vf(XSL[tt % 6], 2048), in_=x_pairs[:, tt:tt + 2, :]).then_inc(sems[f"x{tt % 6}"], 16)
    for pr in range(NTT // 2):
        xcnt[(2 * pr) % 6] = xcnt.get((2 * pr) % 6, 0) + 16

    def spam(t, n, col=0):
        for _ in range(n):
            t.matmul(P[0][:, col:col + 128], ident, b01, start=True, stop=True)

    @blk.tensor
    def _(t):
        W(t, "sm", 112)
        cnt = {0: 0, 2: 0, 4: 0}
        for pr in range(NTT // 2):
            slot = (2 * pr) % 6
            cnt[slot] += 16
            W(t, f"x{slot}", cnt[slot])
            spam(t, 30)
        spam(t, 40)

    e_r2l = [0] * NTT
    for tt in range(NTT):
        slot = (2 * (tt // 2)) % 6
        nld = 16 * ((tt // 2) // 3 + 1)
        xb = vf(XSL[slot] + 4096 * (tt % 2), 1024)

        @blk.vector
        def _(v, tt=tt, xb=xb, slot=slot, nld=nld):
            if tt == 0:
                W(v, "sm", 112)
            W(v, f"x{slot}", nld)
            v.reduce_sum(vf(R1A + 4 * tt, 1), xb, AX.X).then_inc(sems["dve"], 1)
        bump("dve")

        def _sq(s, tt=tt, xb=xb, slot=slot, nld=nld):
            if tt == 0:
                W(s, "sm", 112)
            W(s, f"x{slot}", nld)
            if tt >= 1:
                W(s, "act", tt)
            s.activation(vf(LSQ2, 1024), xb, AF.Square,
                         accum_out=vf(R2A + 4 * tt, 1)).then_inc(sems["act"], 1)
        SDEF.append(_sq)
        bump("act")
        assert ctr["act"] == tt + 1
        e_r2l[tt] = ctr["dve"]

    e_pb_d = 0

    @blk.vector
    def _(v):
        nonlocal e_pb_d
        W(v, "dve", e_r2l[NTT - 1])
        W(v, "act", NTT)
        step_v(v, lambda: v.tensor_scalar(vf(NBA, 16), vf(R1A, 16), 1.0 / C, None, ALU.mult))
        step_v(v, lambda: v.tensor_scalar(vf(INVA, 16), vf(R2A, 16), 1.0 / C, None, ALU.mult))
        step_v(v, lambda: v.tensor_mul(vf(R1A, 16), vf(NBA, 16), vf(NBA, 16)))
        e_pb_d = step_v(v, lambda: v.tensor_sub(vf(R2A, 16), vf(INVA, 16), vf(R1A, 16)))

    def _stats(s, e_pb_d_v=e_pb_d, ln_ev=ctr["act"] + 1):
        W(s, "dve", e_pb_d_v)
        s.activation(vf(LNT, 16), vf(R2A, 16), AF.Ln, bias=epsc).then_inc(sems["act"], 1)
        W(s, "act", ln_ev)
        s.activation(vf(INVA, 16), vf(LNT, 16), AF.Exp, scale=-0.5).then_inc(sems["act"], 1)
    SDEF.append(_stats)
    bump("act"); bump("act")
    e_pb_a = ctr["act"]

    e_nb_all = 0

    @blk.vector
    def _(v):
        nonlocal e_nb_all
        W(v, "act", e_pb_a)
        step_v(v, lambda: v.tensor_mul(vf(NBA, 16), vf(NBA, 16), vf(INVA, 16)))
        e_nb_all = step_v(v, lambda: v.tensor_scalar(vf(NBA, 16), vf(NBA, 16), -1.0, None, ALU.mult))

    @blk.sync
    def _(sy):
        for tt in range(NTT):
            if tt < 2:
                W(sy, "act", NTT)
                W(sy, "dve", NTT)
            elif tt >= 4:
                W(sy, "act", ev_apply[tt - 4])
            sy.dma_start(out=vf(LXB4[tt % 4], 1024), in_=x_tiles[tt]).then_inc(sems[f"x{tt % 4}"], 16)
    xv = [0] * NTT
    for tt in range(NTT):
        xcnt[tt % 4] = xcnt.get(tt % 4, 0) + 16
        ctr[f"x{tt % 4}"] = xcnt[tt % 4]
        xv[tt] = xcnt[tt % 4]

    @blk.gpsimd
    def _(g):
        W(g, "act", ev_apply[NTT - 1])
        g.dma_start(out=vb(WEOFF + 5 * 16384, 8192), in_=we_d[5]).then_inc(sems["we5"], 16)
    ctr["we5"] = 16

    def emit_z(tt):
        zunits = [0] if tt % 2 == 0 else [2]
        pz = P[0] if tt % 2 == 0 else P[1]
        xf32 = vf(XNF32B[tt % 3], 1024)

        @blk.tensor
        def _(t, tt=tt, pz=pz, xf32=xf32, zunits=zunits):
            W(t, "act", ev_evac32[tt])
            wait_units(t, zunits)
            ins = None
            for ct in range(NCT):
                ins = t.matmul(pz[:, 0:8], xf32[:, ct * 128:(ct + 1) * 128],
                               wg32[:, ct * 8:(ct + 1) * 8], start=(ct == 0), stop=(ct == NCT - 1))
            ins.then_inc(sems["pe"], 1)
        e_z0l[tt] = bump("pe")
        touch(zunits, "pe", e_z0l[tt])

        @blk.vector
        def _(v, tt=tt, pz=pz):
            W(v, "pe", e_z0l[tt])
            v.tensor_copy(zall[:, tt * 8:(tt + 1) * 8], pz[:, 0:8]).then_inc(sems["dve"], 1)
        e_zev[tt] = bump("dve")
        touch(zunits, "dve", e_zev[tt])

    def emit_evac32(tt):
        ph = P[2] if tt % 2 == 0 else P[3]
        units = [4, 5] if tt % 2 == 0 else [6, 7]
        xf32 = vf(XNF32B[tt % 3], 1024)

        def _e32(s, tt=tt, ph=ph, xf32=xf32, e_tr=e_trl[tt],
                 e_z=(e_z0l[tt - 3] if tt >= 3 else 0)):
            W(s, "pe", e_tr)
            if tt >= 3:
                W(s, "pe", e_z)
            s.copy(xf32, ph[:]).then_inc(sems["act"], 1)
        SDEF.append(_e32)
        bump("act")
        assert ctr["act"] == ev_evac32[tt], (ctr["act"], ev_evac32[tt])
        touch(units, "act", ev_evac32[tt])

    for tt in range(NTT):
        xb = vf(LXB4[tt % 4], 1024)
        xn_t = vf(XN32B[tt % 3], 1024)
        units = [4, 5] if tt % 2 == 0 else [6, 7]
        ph = P[2] if tt % 2 == 0 else P[3]

        def _ap(s, tt=tt, xb=xb, xn_t=xn_t, e_nb=e_nb_all, xv_t=xv[tt],
                e_tr=(e_trl[tt - 3] if tt >= 3 else 0)):
            if tt == 0:
                W(s, "dve", e_nb)
            W(s, f"x{tt % 4}", xv_t)
            if tt >= 3:
                W(s, "pe", e_tr)
            s.activation(xn_t, xb, AF.Identity, bias=vf(NBA + 4 * tt, 1),
                         scale=vf(INVA + 4 * tt, 1)).then_inc(sems["act"], 1)
        SDEF.append(_ap)
        bump("act")

        @blk.tensor
        def _(t, tt=tt, xn_t=xn_t, ph=ph, units=units):
            if tt == 0:
                W(t, "sm", 112)
            W(t, "act", ev_apply[tt])
            wait_units(t, units)
            ins = None
            for ct in range(NCT):
                ins = t.transpose(ph[:, ct * 128:(ct + 1) * 128], xn_t[:, ct * 128:(ct + 1) * 128], identf)
            ins.then_inc(sems["pe"], 1)
            spam(t, 8, col=512)
        e_trl[tt] = bump("pe")
        touch(units, "pe", e_trl[tt], reset=True)

        @blk.vector
        def _(v, tt=tt, ph=ph):
            W(v, "pe", e_trl[tt])
            v.tensor_copy(xnf3[:, :, tt * 128:(tt + 1) * 128],
                          ph[:, 0:1024].rearrange("p (ct t) -> p ct t", t=128)).then_inc(sems["dve"], 1)
        e_xnf[tt] = bump("dve")
        touch(units, "dve", e_xnf[tt])

        if tt >= 1:
            emit_evac32(tt - 1)
        if tt >= 3:
            emit_z(tt - 3)
    emit_evac32(NTT - 1)
    for tt in (NTT - 3, NTT - 2, NTT - 1):
        emit_z(tt)

    e_wtevac = [0] * NTT
    zv = zall.rearrange("p (tt e) -> p tt e", e=8)
    e_gd = 0

    @blk.vector
    def _(v):
        nonlocal e_gd
        W(v, "dve", e_zev[NTT - 1])
        m1b = vf(M1A, 16).unsqueeze(2).broadcast_to([128, 16, 8])
        m2b = vf(M2A, 16).unsqueeze(2).broadcast_to([128, 16, 8])
        i1b = vf(I1A, 16).unsqueeze(2).broadcast_to([128, 16, 8])
        i2b = vf(I2A, 16).unsqueeze(2).broadcast_to([128, 16, 8])
        iotab = iota8.unsqueeze(1).broadcast_to([128, 16, 8])
        ltv = vf(LTA, 128).rearrange("p (tt e) -> p tt e", e=8)
        penv = vf(PENA, 128).rearrange("p (tt e) -> p tt e", e=8)
        eq1v = vf(EQ1A, 128).rearrange("p (tt e) -> p tt e", e=8)
        eq2v = vf(EQ2A, 128).rearrange("p (tt e) -> p tt e", e=8)
        z2v = vf(Z2A, 128).rearrange("p (tt e) -> p tt e", e=8)
        iotr = vf(IOTR, 128).rearrange("p (tt e) -> p tt e", e=8)
        step_v(v, lambda: v.tensor_copy(iotr, iotab))
        step_v(v, lambda: v.tensor_reduce(vf(M1A, 16), zv, AX.X, ALU.max))
        step_v(v, lambda: v.tensor_tensor(ltv, zv, m1b, ALU.is_lt))
        step_v(v, lambda: v.scalar_tensor_tensor(penv, ltv, BIGC, iotr, ALU.mult, ALU.add))
        step_v(v, lambda: v.tensor_reduce(vf(I1A, 16), penv, AX.X, ALU.min))
        step_v(v, lambda: v.tensor_tensor(eq1v, iotr, i1b, ALU.is_equal))
        step_v(v, lambda: v.scalar_tensor_tensor(z2v, eq1v, -BIGC, zall, ALU.mult, ALU.add))
        step_v(v, lambda: v.tensor_reduce(vf(M2A, 16), z2v, AX.X, ALU.max))
        step_v(v, lambda: v.tensor_tensor(ltv, z2v, m2b, ALU.is_lt))
        step_v(v, lambda: v.scalar_tensor_tensor(penv, ltv, BIGC, iotr, ALU.mult, ALU.add))
        step_v(v, lambda: v.tensor_reduce(vf(I2A, 16), penv, AX.X, ALU.min))
        step_v(v, lambda: v.tensor_tensor(eq2v, iotr, i2b, ALU.is_equal))
        e_gd = step_v(v, lambda: v.tensor_sub(vf(DA, 16), vf(M2A, 16), vf(M1A, 16)))

    def _ge(s, e_gd_v=e_gd):
        W(s, "dve", e_gd_v)
        s.activation(vf(W2A, 16), vf(DA, 16), AF.Exp).then_inc(sems["act"], 1)
    SDEF.append(_ge)
    e_ge = bump("act")

    e_w_all = 0

    @blk.vector
    def _(v):
        nonlocal e_w_all
        W(v, "act", e_ge)
        w1b = vf(W1A, 16).unsqueeze(2).broadcast_to([128, 16, 8])
        w2b = vf(W2A, 16).unsqueeze(2).broadcast_to([128, 16, 8])
        eq1v = vf(EQ1A, 128).rearrange("p (tt e) -> p tt e", e=8)
        eq2v = vf(EQ2A, 128).rearrange("p (tt e) -> p tt e", e=8)
        ltv = vf(LTA, 128).rearrange("p (tt e) -> p tt e", e=8)
        wtv = vf(WTA, 128).rearrange("p (tt e) -> p tt e", e=8)
        step_v(v, lambda: v.tensor_scalar(vf(W1A, 16), vf(W2A, 16), 1.0, None, ALU.add))
        step_v(v, lambda: v.reciprocal(vf(W2A, 16), vf(W1A, 16)))
        step_v(v, lambda: v.tensor_scalar(vf(W1A, 16), vf(W2A, 16), -1.0, 1.0, ALU.mult, ALU.add))
        step_v(v, lambda: v.tensor_tensor(wtv, eq1v, w2b, ALU.mult))
        step_v(v, lambda: v.tensor_tensor(ltv, eq2v, w1b, ALU.mult))
        step_v(v, lambda: v.tensor_add(wall.rearrange("p (tt e) -> p tt e", e=8), wtv, ltv))
        e_w_all = step_v(v, lambda: v.tensor_copy(vb(WBFA, 128), wall))

    @blk.tensor
    def _(t):
        W(t, "dve", e_zev[NTT - 1])
        spam(t, 60, col=512)

    for tt in range(NTT):
        units = [6] if tt % 2 == 0 else [4]
        pz = P[3] if tt % 2 == 0 else P[2]
        pzb = pz[:].bitcast(BF16)

        @blk.tensor
        def _(t, tt=tt, pzb=pzb, units=units):
            if tt == 0:
                W(t, "dve", e_w_all)
            wait_units(t, units)
            t.transpose(pzb[0:8, 32:160], vb(WBFA, 128)[:, tt * 8:(tt + 1) * 8], ident).then_inc(sems["pe"], 1)
        e_wtr = bump("pe")
        touch(units, "pe", e_wtr)

        @blk.vector
        def _(v, tt=tt, pzb=pzb):
            W(v, "pe", e_wtr)
            v.tensor_copy(wt_sb[:, tt * 128:(tt + 1) * 128], pzb[0:8, 32:160]).then_inc(sems["dve"], 1)
        e_wtevac[tt] = bump("dve")
        touch(units, "dve", e_wtevac[tt])

    e_wtr_all = ctr["pe"]

    @blk.gpsimd
    def _(g):
        W(g, "act", ev_evac32[NTT - 1])
        W(g, "pe", e_z0l[NTT - 1])
        W(g, "dve", e_w_all)
        W(g, "pe", e_wtr_all)
        for e in (6, 7):
            g.dma_start(out=vb(WEOFF + e * 16384, 8192), in_=we_d[e]).then_inc(sems[f"we{e}"], 16)
    ctr["we6"] = 16; ctr["we7"] = 16

    if PHASES <= 2:
        return finish_trunc()

    e_silu = [0] * NTT
    xne_guard = [0, 0, 0]
    e_xne_last = [0] * NTT
    e_wbc_l = [0] * NTT
    XNE3 = [WBC, XNE[0], XNE[1]]

    def emit_wbc(tt):
        pwbc = P[2] if tt % 2 == 0 else P[3]
        wbc_units = [4, 5] if tt % 2 == 0 else [6, 7]

        @blk.tensor
        def _(t, tt=tt, pwbc=pwbc, wbc_units=wbc_units):
            W(t, "dve", e_wtevac[tt])
            if tt >= 2:
                W(t, "dve", e_xne_last[tt - 2])
            wait_units(t, wbc_units)
            ins = None
            for e in range(E):
                ins = t.matmul(pwbc[:, e * 128:(e + 1) * 128], sel8[:, e * 128:(e + 1) * 128],
                               wt_sb[:, tt * 128:(tt + 1) * 128], start=True, stop=True)
            ins.then_inc(sems["pe"], 1)
        e_wbc_l[tt] = bump("pe")
        touch(wbc_units, "pe", e_wbc_l[tt])

    emit_wbc(0)
    for tt in range(NTT):
        accP = P[tt % 2]
        acc_units = [0, 1] if tt % 2 == 0 else [2, 3]
        pwbc = P[2] if tt % 2 == 0 else P[3]
        wbc_units = [4, 5] if tt % 2 == 0 else [6, 7]

        for e in range(E):
            @blk.vector
            def _(v, tt=tt, e=e, pwbc=pwbc):
                W(v, "pe", max(e_wbc_l[tt], xne_guard[e % 3]))
                wbc_e = pwbc[:].rearrange("p (e t) -> p e t", t=128)[:, e, :]
                v.tensor_tensor(vb(XNE3[e % 3], 1024).rearrange("p (ct t) -> p ct t", t=128),
                                xnf3[:, :, tt * 128:(tt + 1) * 128],
                                wbc_e.unsqueeze(1).broadcast_to([128, 8, 128]),
                                ALU.mult).then_inc(sems["dve"], 1)
            e_xne = bump("dve")
            if e == E - 1:
                e_xne_last[tt] = e_xne
                touch(wbc_units, "dve", e_xne)

            @blk.tensor
            def _(t, tt=tt, e=e, e_xne=e_xne):
                W(t, "dve", e_xne)
                if tt == 0:
                    W(t, f"we{e}", 16)
                if e == 0:
                    wait_units(t, acc_units)
                we_v = vb(WEOFF + e * 16384, 8192)
                ins = None
                for ct in range(NCT):
                    for hf in range(2):
                        ins = t.matmul(accP[:, hf * 512:(hf + 1) * 512],
                                       vb(XNE3[e % 3], 1024)[:, ct * 128:(ct + 1) * 128],
                                       we_v[:, ct * 1024 + hf * 512: ct * 1024 + (hf + 1) * 512],
                                       start=(e == 0 and ct == 0), stop=False)
                ins.then_inc(sems["pe"], 1)
            xne_guard[e % 3] = bump("pe")
            if e == 0 and tt + 1 < NTT:
                emit_wbc(tt + 1)

        @blk.tensor
        def _(t, tt=tt):
            if tt == 0:
                W(t, "bes", 16)
            ins = None
            for hf in range(2):
                ins = t.matmul(accP[:, hf * 512:(hf + 1) * 512], wt_sb[:, tt * 128:(tt + 1) * 128],
                               be_sb[:, hf * 512:(hf + 1) * 512], start=False, stop=True)
            ins.then_inc(sems["pe"], 1)
        e_acc = bump("pe")
        touch(acc_units, "pe", e_acc)

        def _si(s, tt=tt, accP=accP, e_acc=e_acc):
            W(s, "pe", e_acc)
            s.activation(gsb[:, tt * 1024:(tt + 1) * 1024], accP[:], AF.Silu).then_inc(sems["act"], 1)
        SDEF.append(_si)
        e_silu[tt] = bump("act")
        touch(acc_units, "act", e_silu[tt])

    e_ffn_all = ctr["pe"]

    if PHASES <= 3:
        return finish_trunc()

    @blk.vector
    def _(v):
        W(v, "sm", 112)
        v.tensor_scalar(vf(BQ8, 8), bqpp[:, 0:8], SCALE, None, ALU.mult).then_inc(sems["dve"], 1)
    e_bq8 = bump("dve")

    qsb3 = vb(QOFF, 16384).rearrange("p (ot t) -> p ot t", t=T)
    ksb3 = vb(KOFF, 16384).rearrange("p (ot t) -> p ot t", t=T)
    qsb = vb(QOFF, 16384)
    ksb = vb(KOFF, 16384)
    vsb = vb(VOFF, 16384)
    vsb3 = vb(VOFF, 16384).rearrange("p (tt o) -> p tt o", o=1024)
    wpt = vb(WPT, 8192)
    e_grp = [0] * 6
    slot_i = [0]

    def next_slot():
        u = slot_i[0] % 8
        slot_i[0] += 1
        return u

    for g4 in range(6):
        wq_v = vb(WQT[g4 % 2], 4096)
        if g4 < 4:
            for oi in range(4):
                ot = g4 * 4 + oi
                for chunk in range(4):
                    u = next_slot()
                    ps = P[u // 2][:, (u % 2) * 512:(u % 2) * 512 + 512]

                    @blk.tensor
                    def _(t, g4=g4, oi=oi, chunk=chunk, u=u, ps=ps, wq_v=wq_v):
                        if oi == 0 and chunk == 0:
                            W(t, f"wq{g4 % 2}", ev_wq[g4])
                        wait_units(t, [u])
                        ins = None
                        for ct in range(NCT):
                            ins = t.matmul(ps, wq_v[:, ct * 512 + oi * 128: ct * 512 + (oi + 1) * 128],
                                           xnf3[:, ct, chunk * 512:(chunk + 1) * 512],
                                           start=(ct == 0), stop=(ct == NCT - 1))
                        ins.then_inc(sems["pe"], 1)
                    e_mm = bump("pe")
                    touch([u], "pe", e_mm)

                    if ot < 8:
                        def _qe(s, ot=ot, chunk=chunk, ps=ps, e_mm=e_mm, e_bq8=e_bq8):
                            W(s, "pe", e_mm)
                            if ot == 0 and chunk == 0:
                                W(s, "dve", e_bq8)
                            s.activation(qsb[:, ot * 2048 + chunk * 512: ot * 2048 + (chunk + 1) * 512],
                                         ps, AF.Identity, bias=vf(BQ8 + 4 * ot, 1), scale=SCALE).then_inc(sems["act"], 1)
                        SDEF.append(_qe)
                        touch([u], "act", bump("act"))
                    else:
                        @blk.vector
                        def _(v, ot=ot, chunk=chunk, ps=ps, e_mm=e_mm):
                            W(v, "pe", e_mm)
                            v.tensor_scalar(ksb[:, (ot - 8) * 2048 + chunk * 512:
                                                (ot - 8) * 2048 + (chunk + 1) * 512],
                                            ps, bqpp[:, ot:ot + 1], None, ALU.add).then_inc(sems["dve"], 1)
                        touch([u], "dve", bump("dve"))
        else:
            vh = g4 - 4
            for tt in range(NTT):
                u = next_slot()
                ps = P[u // 2][:, (u % 2) * 512:(u % 2) * 512 + 512]

                @blk.tensor
                def _(t, g4=g4, tt=tt, vh=vh, u=u, ps=ps, wq_v=wq_v):
                    if tt == 0:
                        W(t, f"wq{g4 % 2}", ev_wq[g4])
                        W(t, "vr", 32)
                    wait_units(t, [u])
                    for ct in range(NCT):
                        t.matmul(ps, xnf3[:, ct, tt * 128:(tt + 1) * 128],
                                 wq_v[:, ct * 512:(ct + 1) * 512], start=(ct == 0), stop=False)
                    t.matmul(ps, ones_row, vrow[:, vh * 512:(vh + 1) * 512],
                             start=False, stop=True).then_inc(sems["pe"], 1)
                e_mm = bump("pe")
                touch([u], "pe", e_mm)

                @blk.vector
                def _(v, tt=tt, vh=vh, ps=ps, e_mm=e_mm):
                    W(v, "pe", e_mm)
                    v.tensor_copy(vsb[:, tt * 1024 + vh * 512: tt * 1024 + (vh + 1) * 512],
                                  ps).then_inc(sems["dve"], 1)
                touch([u], "dve", bump("dve"))
        e_grp[g4] = ctr["pe"]

    e_qkv_pe = ctr["pe"]
    e_qkv_dve = ctr["dve"]
    e_qkv_act = ctr["act"]

    @blk.gpsimd
    def _(g):
        W(g, "pe", e_ffn_all)
        for g4 in range(2):
            g.dma_start(out=vb(WQT[g4 % 2], 4096).rearrange("p (ct o) -> p ct o", o=512),
                        in_=wq_r[:, :, g4 * 512:(g4 + 1) * 512]).then_inc(sems[f"wq{g4 % 2}"], 16)
        g.dma_start(out=wpt.rearrange("p (ct o) -> p ct o", o=1024),
                    in_=wp_r[:, :, :]).then_inc(sems["wp"], 16)
        g.dma_start(out=vrow, in_=vrow_d[:, :]).then_inc(sems["vr"], 16)
        g.dma_start(out=prow, in_=prow_d[:, :]).then_inc(sems["vr"], 16)
        for g4 in range(2, 6):
            W(g, "pe", e_grp[g4 - 2])
            g.dma_start(out=vb(WQT[g4 % 2], 4096).rearrange("p (ct o) -> p ct o", o=512),
                        in_=wq_r[:, :, g4 * 512:(g4 + 1) * 512]).then_inc(sems[f"wq{g4 % 2}"], 16)
    ctr["wq0"] = 48; ctr["wq1"] = 48; ctr["wp"] = 16
    ctr["vr"] = 32

    if PHASES <= 4:
        return finish_trunc()

    NJ = 4 * NTT
    p1b = P[1][:].bitcast(BF16)
    p2b = P[2][:].bitcast(BF16)
    aun4 = [vb(AUN4[i], 512) for i in range(4)]
    amskv = vb(AMSK, 512)
    atb2 = [vb(ATB2[0], 512), vb(ATB2[1], 512), vb(ATB2[2], 512)]
    attn4 = vb(ATTN, 2048).rearrange("p (c e t) -> p c e t", c=2, e=8)
    t1b = vb(XNE[1], 1024)

    def spsum(j):
        r = j % 3
        if r == 0:
            return P[0][:, 0:512]
        if r == 1:
            return P[0][:, 512:1024]
        return P[1][:, 0:512]

    def atpsum(j):
        if j % 2 == 0:
            return p1b[:, 1024:2048]
        return p2b[:, 0:1024]

    def avpsum(j):
        return P[3][:, (j % 2) * 512:(j % 2) * 512 + 512]

    prpsum = P[2][:, 512:1024]
    e_s = [0] * NJ
    e_exp = [0] * NJ
    e_norm = [0] * NJ
    e_atr = [0] * NJ
    e_atev = [0] * NJ
    e_av = [0] * NJ
    e_avev = [0] * NJ
    e_projA = [0] * NTT
    e_projB = [0] * NTT
    e_prc1 = [0] * NTT
    e_prc2 = [0] * NTT
    e_cmb1 = [0] * NTT
    e_cmb2 = [0] * NTT
    e_cmb = [0] * NTT
    e_xrs = [0] * NTT

    @blk.sync
    def _(sy):
        W(sy, "pe", e_qkv_pe)
        sy.dma_start(out=vf(XB[0], 1024), in_=x_tiles[0]).then_inc(sems["xa"], 16)
        sy.dma_start(out=vf(XB[1], 1024), in_=x_tiles[1]).then_inc(sems["xb"], 16)
    ctr["xa"] = 16; ctr["xb"] = 16
    e_xrs[0] = 16; e_xrs[1] = 16

    for it in range(NJ + 14):
        js = it
        jm = it - 1
        jn = it - 2
        ja = it - 4
        jv = it - 6
        b_prA = (it - 11) // 4 if (it - 11) >= 0 and (it - 11) % 4 == 0 else -1
        b_cmA = (it - 12) // 4 if (it - 12) >= 0 and (it - 12) % 4 == 0 else -1
        b_prB = (it - 13) // 4 if (it - 13) >= 0 and (it - 13) % 4 == 0 else -1
        b_cmB = (it - 14) // 4 if (it - 14) >= 0 and (it - 14) % 4 == 0 else -1

        @blk.tensor
        def _(t, it=it, js=js, ja=ja, jv=jv, b_prA=b_prA, b_prB=b_prB):
            if 0 <= ja < NJ:
                W(t, "dve", e_norm[ja])
                if ja >= 2:
                    W(t, "act", e_atev[ja - 2])
                ap = atpsum(ja)
                for k in range(4):
                    ins = t.transpose(ap[:, k * 128:(k + 1) * 128],
                                      aun4[ja % 4][:, k * 128:(k + 1) * 128], ident)
                ins.then_inc(sems["pe"], 1)
                e_atr[ja] = bump("pe")
            if 0 <= jv < NJ:
                b, qa = jv // 4, jv % 4
                if jv >= 2:
                    W(t, "act", e_avev[jv - 2])
                else:
                    W(t, "act", e_atev[jv])
                avp = avpsum(jv)
                for k in range(4):
                    h = qa * 4 + k
                    ins = t.matmul(avp[:, k * 128:(k + 1) * 128],
                                   vsb3[:, b, (h // 2) * 128:(h // 2 + 1) * 128],
                                   atb2[jv % 3][:, k * 128:(k + 1) * 128], start=True, stop=True)
                ins.then_inc(sems["pe"], 1)
                e_av[jv] = bump("pe")
            for b_p, hf in ((b_prA, 0), (b_prB, 1)):
                if b_p >= 0 and b_p < NTT:
                    b = b_p
                    if hf == 0:
                        W(t, "act", e_avev[4 * b + 3])
                        if b == 0:
                            W(t, "wp", 16)
                    W(t, "act", e_prc1[b] if hf == 1 else (e_prc2[b - 1] if b >= 1 else 0))
                    for ct in range(NCT):
                        t.matmul(prpsum, attn4[:, b % 2, ct, :],
                                 wpt[:, ct * 1024 + hf * 512: ct * 1024 + (hf + 1) * 512],
                                 start=(ct == 0), stop=False)
                    ins = t.matmul(prpsum, ones_row,
                                   prow[:, hf * 512:(hf + 1) * 512], start=False, stop=True)
                    ins.then_inc(sems["pe"], 1)
                    if hf == 0:
                        e_projA[b] = bump("pe")
                    else:
                        e_projB[b] = bump("pe")
            if js < NJ:
                b, qa = js // 4, js % 4
                if js == 0:
                    W(t, "dve", e_qkv_dve)
                    W(t, "act", e_qkv_act)
                if 3 <= js < 8:
                    W(t, "act", e_exp[js - 3])
                sp = spsum(js)
                t.matmul(sp, ident, qsb[:, 0:512], start=True, stop=True)
                for k in [0, 2]:
                    h = qa * 4 + k
                    t.matmul(sp[:, k * 128:(k + 1) * 128],
                             qsb3[0:64, h // 2, b * 128:(b + 1) * 128],
                             ksb3[0:64, h // 2, b * 128:(b + 1) * 128], start=True, stop=True)
                t.ldweights(ident[:])
                for k in [1, 3]:
                    h = qa * 4 + k
                    ins = t.matmul(sp[:, k * 128:(k + 1) * 128],
                                   qsb3[64:128, h // 2, b * 128:(b + 1) * 128],
                                   ksb3[64:128, h // 2, b * 128:(b + 1) * 128], start=True, stop=True)
                ins.then_inc(sems["pe"], 1)
                e_s[js] = bump("pe")

        if 0 <= jm < NJ:
            @blk.vector
            def _(v, jm=jm):
                W(v, "pe", e_s[jm])
                if jm >= 2:
                    W(v, "act", e_exp[jm - 2])
                v.reduce_max(vf(MNEG2[jm % 2], 1), spsum(jm), AX.X, negate=True).then_inc(sems["dve"], 1)
            e_mx = bump("dve")

        if 0 <= jm < NJ:
            e_exp[jm] = bump("act")

            def _exp(s, jm=jm, e_mx=ctr["dve"],
                     e_atr_v=(e_atr[jm - 4] if 4 <= jm < NJ else 0)):
                W(s, "dve", e_mx)
                if jm >= 4:
                    W(s, "pe", e_atr_v)
                s.activation(aun4[jm % 4], spsum(jm), AF.Exp,
                             bias=vf(MNEG2[jm % 2], 1)).then_inc(sems["act"], 1)
            SDEF.append(_exp)

        if 0 <= ja < NJ:
            e_atev[ja] = bump("act")

            def _atev(s, ja=ja, e_atr_v=e_atr[ja],
                      e_av_w=(e_av[ja - 3] if ja >= 3 else 0)):
                W(s, "pe", e_atr_v)
                if ja >= 3:
                    W(s, "pe", e_av_w)
                s.copy(atb2[ja % 3], atpsum(ja)[:, 0:512]).then_inc(sems["act"], 1)
            SDEF.append(_atev)

        if 0 <= jv < NJ:
            bump("act")
            e_avev[jv] = bump("act")

            def _avev(s, jv=jv, e_av_v=e_av[jv],
                      e_pB=(e_projB[jv // 4 - 2] if jv // 4 >= 2 and jv % 4 == 0 else 0)):
                b, qa = jv // 4, jv % 4
                W(s, "pe", e_av_v)
                if b >= 2 and qa == 0:
                    W(s, "pe", e_pB)
                p2v = avpsum(jv).rearrange("p (e two t) -> p e two t", two=2, t=128)
                ct0 = qa * 2
                s.copy(attn4[0:64, b % 2, ct0:ct0 + 2, :], p2v[0:64, :, 0, :]).then_inc(sems["act"], 1)
                s.copy(attn4[64:128, b % 2, ct0:ct0 + 2, :], p2v[64:128, :, 1, :]).then_inc(sems["act"], 1)
            SDEF.append(_avev)

        for b_p, hf in ((b_prA, 0), (b_prB, 1)):
            if 0 <= b_p < NTT:
                ev = bump("act")
                (e_prc1 if hf == 0 else e_prc2)[b_p] = ev
                def _prc(s, hf=hf,
                         e_pr=(e_projA[b_p] if hf == 0 else e_projB[b_p]),
                         e_war=((e_cmb1[b_p - 1] if hf == 0 else e_cmb2[b_p - 1]) if b_p >= 1 else 0)):
                    W(s, "pe", e_pr)
                    if e_war:
                        W(s, "dve", e_war)
                    s.copy(vf(PRC[hf], 512), prpsum).then_inc(sems["act"], 1)
                SDEF.append(_prc)

        @blk.vector
        def _(v, it=it, jn=jn, ja=ja, b_cmA=b_cmA, b_cmB=b_cmB):
            if 0 <= jn < NJ:
                W(v, "act", e_exp[jn])
                if jn >= 1:
                    W(v, "dve", e_norm[jn - 1])
                v.tensor_tensor(amskv.rearrange("p (e t) -> p e t", t=128),
                                aun4[jn % 4].rearrange("p (e t) -> p e t", t=128),
                                b01.unsqueeze(1).broadcast_to([128, 4, 128]), ALU.mult).then_inc(sems["dve"], 1)
                val = bump("dve"); W(v, "dve", val)
                v.tensor_reduce(vf(R8N, 4), amskv.rearrange("p (e t) -> p e t", t=128),
                                AX.X, ALU.add).then_inc(sems["dve"], 1)
                val = bump("dve"); W(v, "dve", val)
                v.reciprocal(vf(RHON, 4), vf(R8N, 4)).then_inc(sems["dve"], 1)
                val = bump("dve"); W(v, "dve", val)
                ins = None
                for k in range(4):
                    ins = v.tensor_scalar(aun4[jn % 4][:, k * 128:(k + 1) * 128],
                                          amskv[:, k * 128:(k + 1) * 128],
                                          vf(RHON + 4 * k, 1), None, ALU.mult)
                ins.then_inc(sems["dve"], 1)
                e_norm[jn] = bump("dve")
            for b_c, hf in ((b_cmA, 0), (b_cmB, 1)):
                if b_c >= 0 and b_c < NTT:
                    b = b_c
                    W(v, "act", e_prc1[b] if hf == 0 else e_prc2[b])
                    if hf == 0:
                        W(v, "xa" if b % 2 == 0 else "xb", e_xrs[b])
                        if b >= 2:
                            W(v, "so0" if b % 2 == 0 else "so1", 16 * ((b - 2) // 2 + 1))
                        W(v, "act", e_silu[b])
                    v.tensor_tensor(t1b[:, hf * 512:(hf + 1) * 512], vf(PRC[hf], 512),
                                    gsb[:, b * 1024 + hf * 512: b * 1024 + (hf + 1) * 512],
                                    ALU.mult).then_inc(sems["dve"], 1)
                    if hf == 0:
                        e_cmb1[b] = bump("dve")
                    else:
                        e_cmb2[b] = bump("dve")
                    val = ctr["dve"]; W(v, "dve", val)
                    v.tensor_add(vf(OBUF[b % 2] + 2048 * hf, 512),
                                 t1b[:, hf * 512:(hf + 1) * 512],
                                 vf(XB[b % 2] + 2048 * hf, 512)).then_inc(sems["dve"], 1)
                    if hf == 1:
                        e_cmb[b] = bump("dve")
                    else:
                        bump("dve")

        if b_cmB >= 0 and b_cmB < NTT:
            @blk.sync
            def _(sy, b=b_cmB):
                W(sy, "dve", e_cmb[b])
                so_s = "so0" if b % 2 == 0 else "so1"
                sy.dma_start(out=out_tiles[b], in_=vf(OBUF[b % 2], 1024)).then_inc(sems[so_s], 16)
                bump(so_s, 16)
                if b + 2 < NTT:
                    xs = "xa" if b % 2 == 0 else "xb"
                    sy.dma_start(out=vf(XB[b % 2], 1024), in_=x_tiles[b + 2]).then_inc(sems[xs], 16)
                    e_xrs[b + 2] = bump(xs, 16)

    s_flush()

    @blk.sync
    def _(sy):
        W(sy, "so0", ctr["so0"])
        W(sy, "so1", ctr["so1"])
        W(sy, "pe", ctr["pe"]); W(sy, "dve", ctr["dve"]); W(sy, "act", ctr["act"])

    es.close()
    return nc


def _host_inputs(Wqkv, bqkv, Wproj, bproj, Wg, We, be):
    bf = ml_dtypes.bfloat16
    sel = np.zeros((8, 1024), np.float32)
    for e in range(8):
        sel[e, e * 128:(e + 1) * 128] = 1.0
    return {
        "wqkvT": np.ascontiguousarray(Wqkv.T),
        "bqpp": np.ascontiguousarray(bqkv.reshape(24, 128).T),
        "vrow": np.ascontiguousarray(bqkv[None, 2048:]),
        "wprojT": np.ascontiguousarray(Wproj.T),
        "prow": np.ascontiguousarray(bproj[None, :]),
        "wg": np.ascontiguousarray(Wg.reshape(8, 128, 8).transpose(1, 0, 2).reshape(128, 64)),
        "we": np.ascontiguousarray(We.reshape(8, 8, 128, 1024).transpose(0, 2, 1, 3).reshape(8, 128, 8192)),
        "be": np.ascontiguousarray(be),
        "ident": np.eye(128, dtype=bf),
        "identf": np.eye(128, dtype=np.float32),
        "b01": np.kron(np.eye(32, dtype=np.float32), np.ones((4, 4), np.float32)).astype(bf),
        "sel8": sel.astype(bf),
        "iota": np.tile(np.concatenate([np.arange(8, dtype=np.float32), [EPS]]), (128, 1)),
    }


def kernel(L0_attn_x, L0_attn_Wqkv, L0_attn_bqkv, L0_attn_Wproj, L0_attn_bproj,
           L0_ffn_Wg, L0_ffn_We, L0_ffn_be):
    x = np.asarray(L0_attn_x, np.float32)
    shared = _host_inputs(np.asarray(L0_attn_Wqkv, np.float32), np.asarray(L0_attn_bqkv, np.float32),
                          np.asarray(L0_attn_Wproj, np.float32), np.asarray(L0_attn_bproj, np.float32),
                          np.asarray(L0_ffn_Wg, np.float32), np.asarray(L0_ffn_We, np.float32),
                          np.asarray(L0_ffn_be, np.float32))
    nc = build_nc()
    in_maps = []
    for k in range(NCORES):
        m = dict(shared)
        m["x"] = np.ascontiguousarray(x[:, k * 512:(k + 1) * 512, :].transpose(1, 0, 2).reshape(T, C))
        in_maps.append(m)
    res = run_bass_kernel_spmd(nc, in_maps, core_ids=list(range(NCORES)))
    outs = [np.asarray(res.results[k]["out"]).reshape(512, 4, C).transpose(1, 0, 2)
            for k in range(NCORES)]
    return np.ascontiguousarray(np.concatenate(outs, axis=1))
